# version 02
# baseline (speedup 1.0000x reference)
import math
from contextlib import ExitStack

import numpy as np
import ml_dtypes

import concourse.bass as bass
import concourse.mybir as mybir
from concourse.bass_utils import run_bass_kernel_spmd

F32 = mybir.dt.float32
BF16 = mybir.dt.bfloat16
I32 = mybir.dt.int32
AF = mybir.ActivationFunctionType
ALU = mybir.AluOpType

NPBF16 = ml_dtypes.bfloat16

D = 2048
INTER = 8192
NKT = 16
NTAU = 64
HD = 128
NH = 16
NKV = 4
C16 = 1.0 / (math.sqrt(2048.0) * 64.0)
C64 = 1.0 / (math.sqrt(8192.0) * 64.0)
LN_EPS = 1e-5
RMS_EPS = 1e-6
NEG_BIG = -1.0e5
ISQD = 1.0 / math.sqrt(128.0)
TC = 512


def _hadamard(n):
    h = np.array([[1.0]], dtype=np.float32)
    while h.shape[0] < n:
        h = np.block([[h, h], [h, -h]])
    return h.astype(np.float32)



ENGS = ("tensor", "scalar", "vector", "gpsimd", "sync")


class Ctx:

    def __init__(self, kind, sems):
        self.kind = kind
        self.eng = None
        self.sems = sems
        self.counts = {k: 0 for k in sems}
        self.guards = {}

    def _inc(self, instr, inc):
        if inc is None:
            return instr
        sem, n = inc
        self.counts[sem] += n
        if instr is not None:
            instr.then_inc(self.sems[sem], n)
        return instr

    def w(self, kind, sem, cnt=None):
        v = self.counts[sem] if cnt is None else cnt
        if self.kind == kind and v > 0:
            self.eng.wait_ge(self.sems[sem], v)

    def barrier(self):
        for sem, v in self.counts.items():
            if v > 0:
                self.eng.wait_ge(self.sems[sem], v)

    def guard(self, buf, sem):
        self.guards.setdefault(buf, {})[sem] = self.counts[sem]

    def await_guards(self, kind, buf):
        g = self.guards.get(buf)
        if not g:
            return
        if self.kind == kind:
            for sem, v in g.items():
                if v > 0:
                    self.eng.wait_ge(self.sems[sem], v)

    def mm(self, out, lhsT, rhs, start, stop, inc=None):
        i = self.eng.matmul(out, lhsT, rhs, start=start, stop=stop) \
            if self.kind == "tensor" else None
        return self._inc(i, inc)

    def act(self, out, in_, func, bias=0.0, scale=1.0, accum=None, inc=None):
        i = self.eng.activation(out, in_, func, bias=bias, scale=scale,
                                accum_out=accum) \
            if self.kind == "scalar" else None
        return self._inc(i, inc)

    def tt(self, out, a, b, op, inc=None):
        i = self.eng.tensor_tensor(out, a, b, op) \
            if self.kind == "vector" else None
        return self._inc(i, inc)

    def ts(self, out, a, s1, op0, s2=None, op1=ALU.bypass, inc=None):
        i = self.eng.tensor_scalar(out, a, s1, s2, op0, op1) \
            if self.kind == "vector" else None
        return self._inc(i, inc)

    def stt(self, out, in0, scalar, in1, op0, op1, inc=None):
        i = self.eng.scalar_tensor_tensor(out, in0, scalar, in1, op0, op1) \
            if self.kind == "vector" else None
        return self._inc(i, inc)

    def vcopy(self, out, in_, inc=None):
        i = self.eng.tensor_copy(out, in_) if self.kind == "vector" else None
        return self._inc(i, inc)

    def recip(self, out, in_, inc=None):
        i = self.eng.reciprocal(out, in_) if self.kind == "vector" else None
        return self._inc(i, inc)

    def memset(self, ap, v, inc=None):
        i = self.eng.memset(ap, v) if self.kind == "vector" else None
        return self._inc(i, inc)

    def dma(self, queue, out, in_, inc=None, slow=False):
        i = None
        if self.kind == queue:
            kw = {"allow_slow_non_contiguous": True} if slow else {}
            i = self.eng.dma_start(out=out, in_=in_, **kw)
        return self._inc(i, inc)


class Builder:

    def __init__(self, nc, sems):
        self.nc = nc
        self.ctxs = {k: Ctx(k, sems) for k in ENGS}

    def block(self, program):
        with self.nc.Block() as blk:
            for kind in ENGS:
                ctx = self.ctxs[kind]

                def mk(ctx):
                    def f(eng):
                        ctx.eng = eng
                        program(ctx)
                    return f
                getattr(blk, kind)(mk(ctx))



SEM_NAMES = ("pe", "act", "dve", "pre", "sv", "sto",
             "gla", "glb", "gsa", "gsb", "wla", "wlb", "wfa", "wfb",
             "hla", "hlb", "rla", "rlb", "sta", "stb")


def alloc_common(nc, stk):
    t = {}

    def sb(name, shape, dt):
        t[name] = stk.enter_context(nc.sbuf_tensor(name, shape, dt))
        return t[name]

    sb("H128c", [128, 128], BF16)
    sb("BD16p", [128, 128], F32)
    sb("BD64p", [128, 128], F32)
    sb("BD16", [128, 16 * 128], BF16)
    sb("BD64", [128, 64 * 128], BF16)
    sb("ones128f", [128, 1], F32)
    sb("ones1f", [1, 128], F32)
    sb("c64row", [1, 128], F32)
    sb("c32row", [1, 128], F32)
    sb("su_ug", [128, 16], F32)
    sb("su_dp", [128, 64], F32)
    sb("kvals", [128, 8], F32)
    return t


def alloc_psum(nc, stk, t):
    def ps(name, shape):
        t[name] = stk.enter_context(nc.psum_tensor(name, shape, F32))
        return t[name]
    t["psA"] = [ps("psA0", [128, 512]), ps("psA1", [128, 512])]
    t["psB"] = [ps("psB0", [128, 512]), ps("psB1", [128, 512])]
    t["psC"] = [ps("psC0", [128, 512]), ps("psC1", [128, 512])]
    t["psD"] = [ps("psD0", [128, 512]), ps("psD1", [128, 512])]
    t["ps8"] = t["psA"] + t["psB"] + t["psC"] + t["psD"]


def alloc_fold(nc, stk, t):
    def sb(name, shape, dt):
        t[name] = stk.enter_context(nc.sbuf_tensor(name, shape, dt))
        return t[name]
    sb("sbc", [128, INTER], F32)
    t["win"] = [sb("win0", [128, INTER], BF16), sb("win1", [128, INTER], BF16)]
    t["wout"] = [sb("wout0", [128, INTER], BF16),
                 sb("wout1", [128, INTER], BF16)]
    sb("svbuf", [1, 512], F32)


def alloc_ffn(nc, stk, t):
    def sb(name, shape, dt):
        t[name] = stk.enter_context(nc.sbuf_tensor(name, shape, dt))
        return t[name]
    sb("u", [128, NTAU * TC], BF16)
    t["xn"] = t["u"][:, :NKT * TC]
    sb("z", [128, NKT * TC], BF16)
    sb("y", [128, TC], BF16)
    sb("gsil", [128, TC], F32)
    sb("stat", [1, 4 * TC], F32)
    sb("rbc", [128, TC], F32)
    sb("mrbc", [128, TC], F32)
    t["hbuf"] = [sb("hbuf0", [128, TC], F32), sb("hbuf1", [128, TC], F32)]
    t["hsq"] = [sb("hsq0", [128, TC], F32), sb("hsq1", [128, TC], F32)]
    t["wu_sl"] = [sb("wu_sl0", [128, 16 * 128], BF16),
                  sb("wu_sl1", [128, 16 * 128], BF16)]
    t["wg_sl"] = [sb("wg_sl0", [128, 16 * 128], BF16),
                  sb("wg_sl1", [128, 16 * 128], BF16)]
    t["wf_sl"] = [sb("wf_sl0", [128, 8 * 128], BF16),
                  sb("wf_sl1", [128, 8 * 128], BF16)]
    t["rbuf"] = [sb("rbuf0", [128, 512], F32), sb("rbuf1", [128, 512], F32)]
    t["hnx"] = [sb("hnx0", [128, 512], F32), sb("hnx1", [128, 512], F32)]


def emit_load_consts(c, io, t):
    c.dma("gpsimd", t["H128c"][:, :], io["H128c_i"].ap(), inc=("pre", 16))
    c.dma("gpsimd", t["BD16p"][:, :], io["BD16p_i"].ap(), inc=("pre", 16))
    c.dma("gpsimd", t["BD64p"][:, :], io["BD64p_i"].ap(), inc=("pre", 16))
    c.dma("gpsimd", t["su_ug"][:, :], io["su_ug_cols"].ap(), inc=("pre", 16))
    c.dma("gpsimd", t["su_dp"][:, :], io["su_dp_cols"].ap(), inc=("pre", 16))
    c.memset(t["ones128f"][:, :], 1.0)
    c.memset(t["ones1f"][:1, :], 1.0)
    c.memset(t["c64row"][:1, :], 64.0)
    c.memset(t["kvals"][:, 0:1], LN_EPS)
    c.memset(t["kvals"][:, 1:2], RMS_EPS)
    c.memset(t["kvals"][:, 2:3], 50.0)
    c.memset(t["kvals"][:, 3:4], 1.0 / 50.0)
    c.memset(t["c32row"][:1, :], 32.0, inc=("dve", 1))
    c.w("vector", "pre")
    for k in range(16):
        c.ts(t["BD16"][:, k * 128:(k + 1) * 128], t["BD16p"][:, :],
             t["su_ug"][:, k:k + 1], ALU.mult)
    for k in range(64):
        c.ts(t["BD64"][:, k * 128:(k + 1) * 128], t["BD64p"][:, :],
             t["su_dp"][:, k:k + 1], ALU.mult,
             inc=("dve", 1) if k == 63 else None)


def emit_weight_fold(c, io, t):
    def fold(src_h, dst_h, sv_h, n512, n_slabs, scale_lhs, dst_ap_fn):
        for i in range(n512):
            c.w("gpsimd", "pe")
            c.dma("gpsimd", t["svbuf"][:1, :],
                  bass.AP(io[sv_h], i * 512, [[512, 1], [1, 512]]),
                  inc=("sv", 16))
            c.w("tensor", "pre")
            c.w("tensor", "sv")
            c.w("tensor", "dve")
            c.mm(t["psD"][0][:, :], t[scale_lhs][:1, :], t["svbuf"][:1, :],
                 True, True, inc=("pe", 1))
            c.w("vector", "pe")
            c.vcopy(t["sbc"][:, i * 512:(i + 1) * 512], t["psD"][0][:, :],
                    inc=("dve", 1))
        c.w("vector", "dve")
        for sl in range(n_slabs):
            bi = sl % 2
            gl = "gla" if bi == 0 else "glb"
            gs = "gsa" if bi == 0 else "gsb"
            c.await_guards("gpsimd", "win" + str(bi))
            c.dma("gpsimd", t["win"][bi][:, :n512 * 512],
                  bass.AP(io[src_h], sl * 128 * n512 * 512,
                          [[n512 * 512, 128], [1, n512 * 512]]),
                  inc=(gl, 16))
            c.w("tensor", gl)
            c.w("tensor", "dve")
            for n in range(n512):
                c.w("tensor", "dve")
                c.mm(t["psD"][n % 2][:, :], t["H128c"][:, :],
                     t["win"][bi][:, n * 512:(n + 1) * 512], True, True,
                     inc=("pe", 1))
                c.w("vector", "pe")
                c.await_guards("vector", "wout" + str(bi))
                c.tt(t["wout"][bi][:, n * 512:(n + 1) * 512],
                     t["psD"][n % 2][:, :],
                     t["sbc"][:, n * 512:(n + 1) * 512], ALU.mult,
                     inc=("dve", 1))
            c.guard("win" + str(bi), "pe")
            c.w("gpsimd", "dve")
            c.dma("gpsimd", dst_ap_fn(sl), t["wout"][bi][:, :n512 * 512],
                  inc=(gs, 16))
            c.guard("wout" + str(bi), gs)

    fold("Wu_st", "WU_hbm", "ws_up", 16, 16, "c64row",
         lambda sl: bass.AP(io["WU_hbm"], sl * 128,
                            [[2048, 128], [262144, 64], [1, 128]]))
    fold("Wg_st", "WG_hbm", "ws_gate", 16, 16, "c64row",
         lambda sl: bass.AP(io["WG_hbm"], sl * 128,
                            [[2048, 128], [262144, 64], [1, 128]]))
    fold("Wd_st", "WF_hbm", "ws_dn", 4, 64, "c32row",
         lambda sl: bass.AP(io["WF_hbm"], sl * 128,
                            [[8192, 128], [1048576, 16], [1, 128]]))


def emit_ln(c, t, h_ap_fn, out_fn):
    c.w("tensor", "dve")
    for kt in range(NKT):
        bi = kt % 2
        hl = "hla" if bi == 0 else "hlb"
        c.await_guards("gpsimd", "hbuf" + str(bi))
        c.dma("gpsimd", t["hbuf"][bi][:, :], h_ap_fn(kt), inc=(hl, 16))
        c.w("scalar", hl)
        c.await_guards("scalar", "hsq" + str(bi))
        c.act(t["hsq"][bi][:, :], t["hbuf"][bi][:, :], AF.Square,
              inc=("act", 1))
        c.w("tensor", "act")
        c.w("tensor", hl)
        c.mm(t["psA"][0][:1, :], t["ones128f"][:, :], t["hbuf"][bi][:, :],
             start=(kt == 0), stop=(kt == NKT - 1))
        c.mm(t["psB"][0][:1, :], t["ones128f"][:, :], t["hsq"][bi][:, :],
             start=(kt == 0), stop=(kt == NKT - 1), inc=("pe", 1))
        c.guard("hbuf" + str(bi), "pe")
        c.guard("hsq" + str(bi), "pe")
    st = t["stat"]
    c.w("vector", "pe")
    c.ts(st[:1, 0:TC], t["psA"][0][:1, :], 1.0 / 2048.0, ALU.mult,
         inc=("dve", 1))
    c.ts(st[:1, TC:2 * TC], t["psB"][0][:1, :], 1.0 / 2048.0, ALU.mult,
         inc=("dve", 1))
    c.w("vector", "dve")
    c.tt(st[:1, 2 * TC:3 * TC], st[:1, 0:TC], st[:1, 0:TC], ALU.mult,
         inc=("dve", 1))
    c.w("vector", "dve")
    c.tt(st[:1, 2 * TC:3 * TC], st[:1, TC:2 * TC], st[:1, 2 * TC:3 * TC],
         ALU.subtract, inc=("dve", 1))
    c.w("scalar", "dve")
    c.act(st[:1, 2 * TC:3 * TC], st[:1, 2 * TC:3 * TC], AF.Sqrt,
          bias=t["kvals"][:1, 0:1], inc=("act", 1))
    c.w("vector", "act")
    c.recip(st[:1, 2 * TC:3 * TC], st[:1, 2 * TC:3 * TC], inc=("dve", 1))
    c.w("vector", "dve")
    c.tt(st[:1, 3 * TC:4 * TC], st[:1, 0:TC], st[:1, 2 * TC:3 * TC],
         ALU.mult, inc=("dve", 1))
    c.w("tensor", "dve")
    c.mm(t["psA"][0][:, :], t["ones1f"][:1, :], st[:1, 2 * TC:3 * TC],
         True, True)
    c.mm(t["psB"][0][:, :], t["ones1f"][:1, :], st[:1, 3 * TC:4 * TC],
         True, True, inc=("pe", 1))
    c.w("vector", "pe")
    c.vcopy(t["rbc"][:, :], t["psA"][0][:, :], inc=("dve", 1))
    c.vcopy(t["mrbc"][:, :], t["psB"][0][:, :], inc=("dve", 1))
    c.w("vector", "pe")
    for kt in range(NKT):
        bi = kt % 2
        hl = "hla" if bi == 0 else "hlb"
        c.await_guards("gpsimd", "hbuf" + str(bi))
        c.dma("gpsimd", t["hbuf"][bi][:, :], h_ap_fn(kt), inc=(hl, 16))
        c.w("vector", hl)
        c.w("vector", "dve")
        c.await_guards("vector", "hsq" + str(bi))
        c.tt(t["hsq"][bi][:, :], t["hbuf"][bi][:, :], t["rbc"][:, :],
             ALU.mult, inc=("dve", 1))
        c.w("vector", "dve")
        c.tt(out_fn(kt), t["hsq"][bi][:, :], t["mrbc"][:, :], ALU.subtract,
             inc=("dve", 1))
        c.guard("hbuf" + str(bi), "dve")
        c.guard("hsq" + str(bi), "dve")


def emit_ffn_chunk(c, io, t, T, row_b, h_in, h_out, dbg=None):

    def h_ap(kt):
        return bass.AP(io[h_in], kt * 128 * T + row_b, [[T, 128], [1, TC]])

    def dump(name, src_ap):
        if dbg is None or name not in dbg:
            return
        c.w("gpsimd", "dve")
        c.w("gpsimd", "act")
        c.w("gpsimd", "pe")
        c.dma("gpsimd", dbg[name].ap(), src_ap, inc=("sto", 16))

    c.w("vector", "pe")
    emit_ln(c, t, h_ap, lambda kt: t["xn"][:, kt * TC:(kt + 1) * TC])
    dump("xn", t["xn"][:, :])

    c.w("tensor", "pre")
    for kt in range(NKT):
        pb = kt % 2
        c.w("tensor", "act")
        c.w("tensor", "dve")
        c.mm(t["psA"][pb][:, :], t["BD16"][:, kt * 128:(kt + 1) * 128],
             t["xn"][:, kt * TC:(kt + 1) * TC], True, True, inc=("pe", 1))
        c.w("scalar", "pe")
        c.act(t["z"][:, kt * TC:(kt + 1) * TC], t["psA"][pb][:, :], AF.Copy,
              inc=("act", 1))
    dump("z", t["z"][:, :])

    for tau in range(NTAU):
        bi = tau % 2
        wl = "wla" if bi == 0 else "wlb"
        c.await_guards("sync", "wu" + str(bi))
        c.dma("sync", t["wu_sl"][bi][:, :],
              bass.AP(io["WU_hbm"], tau * 16 * 128 * 128,
                      [[128, 128], [16384, 16], [1, 128]]),
              inc=(wl, 16))
        c.dma("sync", t["wg_sl"][bi][:, :],
              bass.AP(io["WG_hbm"], tau * 16 * 128 * 128,
                      [[128, 128], [16384, 16], [1, 128]]),
              inc=(wl, 16))
        c.w("tensor", wl)
        c.w("tensor", "act")
        c.w("tensor", "dve")
        for kt in range(NKT):
            zs = t["z"][:, kt * TC:(kt + 1) * TC]
            c.mm(t["psA"][0][:, :], t["wu_sl"][bi][:, kt * 128:(kt + 1) * 128],
                 zs, start=(kt == 0), stop=(kt == NKT - 1))
            c.mm(t["psB"][0][:, :], t["wg_sl"][bi][:, kt * 128:(kt + 1) * 128],
                 zs, start=(kt == 0), stop=(kt == NKT - 1),
                 inc=("pe", 1) if kt == NKT - 1 else None)
        c.guard("wu" + str(bi), "pe")
        c.w("scalar", "pe")
        c.w("scalar", "dve")
        c.act(t["gsil"][:, :], t["psB"][0][:, :], AF.Silu, inc=("act", 1))
        c.w("vector", "act")
        c.tt(t["y"][:, :], t["psA"][0][:, :], t["gsil"][:, :], ALU.mult,
             inc=("dve", 1))
        c.w("tensor", "dve")
        c.mm(t["psC"][0][:, :], t["BD64"][:, tau * 128:(tau + 1) * 128],
             t["y"][:, :], True, True, inc=("pe", 1))
        c.w("scalar", "pe")
        c.act(t["u"][:, tau * TC:(tau + 1) * TC], t["psC"][0][:, :], AF.Copy,
              inc=("act", 1))

    dump("u", t["u"][:, :])

    for half in range(2):
        for t2 in range(NTAU):
            bi = t2 % 2
            wf = "wfa" if bi == 0 else "wfb"
            c.await_guards("sync", "wf" + str(bi))
            c.dma("sync", t["wf_sl"][bi][:, :],
                  bass.AP(io["WF_hbm"], half * 8 * 1048576 + t2 * 128 * 128,
                          [[128, 128], [1048576, 8], [1, 128]]),
                  inc=(wf, 16))
            c.w("tensor", wf)
            if t2 == 0:
                c.w("tensor", "dve")
                c.w("tensor", "act")
            us = t["u"][:, t2 * TC:(t2 + 1) * TC]
            for cc in range(8):
                c.mm(t["ps8"][cc][:, :],
                     t["wf_sl"][bi][:, cc * 128:(cc + 1) * 128], us,
                     start=(t2 == 0), stop=(t2 == NTAU - 1),
                     inc=("pe", 1) if cc == 7 else None)
            c.guard("wf" + str(bi), "pe")
        c.w("gpsimd", "pe")
        c.w("vector", "pe")
        for cc in range(8):
            co = half * 8 + cc
            bi2 = cc % 2
            rl = "rla" if bi2 == 0 else "rlb"
            st_ = "sta" if bi2 == 0 else "stb"
            c.await_guards("gpsimd", "rbuf" + str(bi2))
            c.dma("gpsimd", t["rbuf"][bi2][:, :],
                  bass.AP(io[h_in], co * 128 * T + row_b, [[T, 128], [1, TC]]),
                  inc=(rl, 16))
            c.w("vector", rl)
            c.await_guards("vector", "hnx" + str(bi2))
            c.tt(t["hnx"][bi2][:, :], t["ps8"][cc][:, :], t["rbuf"][bi2][:, :],
                 ALU.add, inc=("dve", 1))
            c.guard("rbuf" + str(bi2), "dve")
            c.w("gpsimd", "dve")
            c.dma("gpsimd",
                  bass.AP(io[h_out], co * 128 * T + row_b, [[T, 128], [1, TC]]),
                  t["hnx"][bi2][:, :], inc=(st_, 16))
            c.guard("hnx" + str(bi2), st_)



def declare_ffn_io(nc, T, debug_w=False):
    io = {}
    io["h_in"] = nc.dram_tensor("h_in", [D, T], F32, kind="ExternalInput")
    io["h_out"] = nc.dram_tensor("h_out", [D, T], F32, kind="ExternalOutput")
    for nm, shp in (("Wu_st", [16, 128, INTER]), ("Wg_st", [16, 128, INTER]),
                    ("Wd_st", [64, 128, D])):
        io[nm] = nc.dram_tensor(nm, shp, F32, kind="ExternalInput")
    for nm, n in (("ws_up", INTER), ("ws_gate", INTER), ("ws_dn", D)):
        io[nm] = nc.dram_tensor(nm, [1, n], F32, kind="ExternalInput")
    io["su_ug_cols"] = nc.dram_tensor("su_ug_cols", [128, 16], F32,
                                      kind="ExternalInput")
    io["su_dp_cols"] = nc.dram_tensor("su_dp_cols", [128, 64], F32,
                                      kind="ExternalInput")
    io["H128c_i"] = nc.dram_tensor("H128c_i", [128, 128], BF16,
                                   kind="ExternalInput")
    io["BD16p_i"] = nc.dram_tensor("BD16p_i", [128, 128], F32,
                                   kind="ExternalInput")
    io["BD64p_i"] = nc.dram_tensor("BD64p_i", [128, 128], F32,
                                   kind="ExternalInput")
    kw = {"kind": "ExternalOutput"} if debug_w else {}
    io["WU_hbm"] = nc.dram_tensor("WU_hbm", [NTAU * 16 * 128 * 128], BF16, **kw)
    io["WG_hbm"] = nc.dram_tensor("WG_hbm", [NTAU * 16 * 128 * 128], BF16, **kw)
    io["WF_hbm"] = nc.dram_tensor("WF_hbm", [16 * NTAU * 128 * 128], BF16, **kw)
    return io


def build_ffn_program(T, debug_w=False, debug_mid=False):
    nc = bass.Bass("TRN2", target_bir_lowering=False)
    stk = ExitStack()
    io = declare_ffn_io(nc, T, debug_w)
    sems = {n: stk.enter_context(nc.semaphore(n)) for n in SEM_NAMES}
    t = alloc_common(nc, stk)
    alloc_psum(nc, stk, t)
    b = Builder(nc, sems)

    fold_stk = ExitStack()
    alloc_fold(nc, fold_stk, t)

    def prog_fold(c):
        emit_load_consts(c, io, t)
        emit_weight_fold(c, io, t)

    b.block(prog_fold)
    fold_stk.close()

    alloc_ffn(nc, stk, t)

    dbg = None
    if debug_mid:
        dbg = {
            "xn": nc.dram_tensor("dbg_xn", [128, NKT * TC], BF16,
                                 kind="ExternalOutput"),
            "z": nc.dram_tensor("dbg_z", [128, NKT * TC], BF16,
                                kind="ExternalOutput"),
            "u": nc.dram_tensor("dbg_u", [128, NTAU * TC], BF16,
                                kind="ExternalOutput"),
        }

    def prog_main(c):
        c.barrier()
        for tc in range(T // TC):
            emit_ffn_chunk(c, io, t, T, tc * TC, "h_in", "h_out",
                           dbg=dbg if tc == 0 else None)
        c.w("gpsimd", "sta")
        c.w("gpsimd", "stb")
        c.w("gpsimd", "sto")

    b.block(prog_main)
    stk.close()
    return nc



def stage_ffn_inputs(ins):
    H16 = _hadamard(16)
    H64 = _hadamard(64)
    H128 = _hadamard(128)
    W_up = np.asarray(ins["L0_ffn_W_up"], dtype=np.float32)
    W_gate = np.asarray(ins["L0_ffn_W_gate"], dtype=np.float32)
    W_down = np.asarray(ins["L0_ffn_W_down"], dtype=np.float32)
    ws_ug = np.asarray(ins["L0_ffn_Wscale_ug"], dtype=np.float32)
    return {
        "Wu_st": np.ascontiguousarray(
            W_up.reshape(INTER, 128, 16).transpose(2, 1, 0)),
        "Wg_st": np.ascontiguousarray(
            W_gate.reshape(INTER, 128, 16).transpose(2, 1, 0)),
        "Wd_st": np.ascontiguousarray(
            W_down.reshape(D, 128, 64).transpose(2, 1, 0)),
        "ws_up": np.ascontiguousarray(ws_ug[:INTER].reshape(1, INTER)),
        "ws_gate": np.ascontiguousarray(ws_ug[INTER:].reshape(1, INTER)),
        "ws_dn": np.ascontiguousarray(
            np.asarray(ins["L0_ffn_Wscale_dp"], np.float32).reshape(1, D)),
        "su_ug_cols": np.ascontiguousarray(
            np.asarray(ins["L0_ffn_SU_ug"], np.float32).reshape(16, 128).T),
        "su_dp_cols": np.ascontiguousarray(
            np.asarray(ins["L0_ffn_SU_dp"], np.float32).reshape(64, 128).T),
        "H128c_i": H128.astype(NPBF16),
        "BD16p_i": np.ascontiguousarray(
            np.kron(np.eye(8, dtype=np.float32), H16) * C16),
        "BD64p_i": np.ascontiguousarray(
            np.kron(np.eye(2, dtype=np.float32), H64) * C64),
    }



ATTN_SEMS = SEM_NAMES + ("vla", "vlb", "rpa", "rpb")


def alloc_attn_common(nc, stk):
    t = {}

    def sb(name, shape, dt):
        t[name] = stk.enter_context(nc.sbuf_tensor(name, shape, dt))
        return t[name]

    sb("ones128f", [128, 1], F32)
    sb("ones1f", [1, 128], F32)
    sb("ones128b", [128, 1], BF16)
    sb("kvals", [128, 8], F32)
    sb("qn_ws", [64, 2], F32)
    sb("kn_ws", [64, 2], F32)
    sb("qn_wn", [64, 2], F32)
    sb("kn_wn", [64, 2], F32)
    sb("Ksb", [128, 4 * 8192], BF16)
    sb("q_n", [128, 16 * 2048], BF16)
    return t


def alloc_attn_prep(nc, stk, t):
    def sb(name, shape, dt):
        t[name] = stk.enter_context(nc.sbuf_tensor(name, shape, dt))
        return t[name]
    sb("a_sb", [128, 16 * 512], BF16)
    sb("stat", [1, 4 * TC], F32)
    sb("rbc", [128, TC], F32)
    sb("mrbc", [128, TC], F32)
    t["hbuf"] = [sb("hbuf0", [128, TC], F32), sb("hbuf1", [128, TC], F32)]
    t["hsq"] = [sb("hsq0", [128, TC], F32), sb("hsq1", [128, TC], F32)]
    t["wsl"] = [sb("wsl0", [128, 16 * 128], BF16),
                sb("wsl1", [128, 16 * 128], BF16)]
    t["wvb"] = [sb("wvb0", [128, 512], BF16), sb("wvb1", [128, 512], BF16)]
    t["cosb"] = sb("cosb", [64, 512], F32)
    t["sinb"] = sb("sinb", [64, 512], F32)
    sb("kn0a", [64, 512], F32)
    sb("kn0b", [64, 512], F32)
    t["rp"] = [sb("rp1", [64, 512], F32), sb("rp2", [64, 512], F32),
               sb("rp3", [64, 512], F32), sb("rp4", [64, 512], F32)]
    sb("rrow", [1, 512], F32)
    sb("vsb", [128, 512], BF16)


def alloc_attn_core(nc, stk, t):
    def sb(name, shape, dt):
        t[name] = stk.enter_context(nc.sbuf_tensor(name, shape, dt))
        return t[name]
    sb("Vsl", [128, 64 * 128], BF16)
    sb("negI", [128, 128], BF16)
    sb("bo", [128, 16], F32)
    sb("Rm", [128, 16 * 512], BF16)
    t["Tsb"] = [sb("Tsb0", [128, 512], F32), sb("Tsb1", [128, 512], F32)]
    t["Psb"] = [sb("Psb0", [128, 512], BF16), sb("Psb1", [128, 512], BF16)]
    sb("y_m", [128, 16 * 512], BF16)
    sb("lrow", [1, 512], F32)
    sb("rbcs", [128, 512], F32)
    t["wsl"] = [sb("wslA", [128, 16 * 128], BF16),
                sb("wslB", [128, 16 * 128], BF16)]
    t["h1t"] = [sb("h1t0", [128, 512], F32), sb("h1t1", [128, 512], F32)]
    t["h2t"] = [sb("h2t0", [128, 512], F32), sb("h2t1", [128, 512], F32)]


def emit_attn_consts(c, io, t):
    c.dma("gpsimd", t["qn_ws"][:, :], io["qn_w"].ap(), inc=("pre", 16))
    c.dma("gpsimd", t["kn_ws"][:, :], io["kn_w"].ap(), inc=("pre", 16))
    c.memset(t["ones128f"][:, :], 1.0)
    c.memset(t["ones1f"][:1, :], 1.0)
    c.memset(t["ones128b"][:, :], 1.0)
    c.memset(t["kvals"][:, 0:1], LN_EPS)
    c.memset(t["kvals"][:, 1:2], RMS_EPS)
    c.memset(t["kvals"][:, 2:3], 50.0)
    c.memset(t["kvals"][:, 3:4], ISQD / 50.0)
    c.memset(t["kvals"][:, 4:5], 1.0 / 128.0, inc=("dve", 1))
    c.w("vector", "pre")
    c.w("vector", "dve")
    c.ts(t["qn_wn"][:, :], t["qn_ws"][:, :], -1.0, ALU.mult)
    c.ts(t["kn_wn"][:, :], t["kn_ws"][:, :], -1.0, ALU.mult, inc=("dve", 1))


def emit_norm_rope(c, t, ps_src, w, wn, cos_ap, sin_ap, out_lo, out_hi):
    rp = t["rp"]
    c.act(t["hsq"][0][:, :], ps_src, AF.Square, inc=("act", 1))
    c.w("tensor", "act")
    c.mm(t["psB"][0][:1, :], t["ones128f"][:, :], t["hsq"][0][:, :],
         True, True, inc=("pe", 1))
    c.w("vector", "pe")
    c.ts(t["rrow"][:1, :], t["psB"][0][:1, :], 1.0 / 128.0, ALU.mult,
         inc=("dve", 1))
    c.w("scalar", "dve")
    c.act(t["rrow"][:1, :], t["rrow"][:1, :], AF.Sqrt, inc=("act", 1))
    c.w("vector", "act")
    c.ts(t["rrow"][:1, :], t["rrow"][:1, :], RMS_EPS, ALU.add, inc=("dve", 1))
    c.w("vector", "dve")
    c.recip(t["rrow"][:1, :], t["rrow"][:1, :], inc=("dve", 1))
    c.w("tensor", "dve")
    c.mm(t["psB"][1][:, :], t["ones1f"][:1, :], t["rrow"][:1, :], True, True,
         inc=("pe", 1))
    c.w("vector", "pe")
    c.vcopy(t["rbc"][:, :512], t["psB"][1][:, :], inc=("dve", 1))
    c.w("vector", "dve")
    c.tt(t["kn0a"][:, :], ps_src[0:64, :], t["rbc"][0:64, :512], ALU.mult,
         inc=("dve", 1))
    c.tt(t["kn0b"][:, :], ps_src[64:128, :], t["rbc"][64:128, :512], ALU.mult,
         inc=("dve", 1))
    c.w("vector", "dve")
    k1 = t["kn0a"][:, :]
    k2 = t["kn0b"][:, :]
    c.tt(rp[0][:, :], k1, cos_ap, ALU.mult, inc=("dve", 1))
    c.tt(rp[1][:, :], k2, sin_ap, ALU.mult, inc=("dve", 1))
    c.w("vector", "dve")
    c.ts(rp[2][:, :], rp[1][:, :], w[:, 1:2], ALU.mult, inc=("dve", 1))
    c.w("vector", "dve")
    c.stt(out_lo, rp[0][:, :], w[:, 0:1], rp[2][:, :], ALU.mult, ALU.add,
          inc=("dve", 1))
    c.tt(rp[1][:, :], k2, cos_ap, ALU.mult, inc=("dve", 1))
    c.tt(rp[3][:, :], k1, sin_ap, ALU.mult, inc=("dve", 1))
    c.w("vector", "dve")
    c.ts(rp[2][:, :], rp[1][:, :], w[:, 1:2], ALU.mult, inc=("dve", 1))
    c.w("vector", "dve")
    c.stt(out_hi, rp[3][:, :], wn[:, 0:1], rp[2][:, :], ALU.mult, ALU.add,
          inc=("dve", 1))


def emit_attn_prep(c, io, t):

    for g in range(16):
        def h_ap(kt, g=g):
            return bass.AP(io["h_full"], kt * 128 * T_FULL_B + g * 512,
                           [[T_FULL_B, 128], [1, TC]])
        emit_ln(c, t, h_ap,
                lambda kt: t["a_sb"][:, kt * 512:(kt + 1) * 512])
        rp_s = "rpa" if g % 2 == 0 else "rpb"
        c.await_guards("gpsimd", "ropeb")
        c.dma("gpsimd", t["cosb"][:, :],
              bass.AP(io["cosk_i"], g * 512, [[8192, 64], [1, 512]]),
              inc=(rp_s, 16))
        c.dma("gpsimd", t["sinb"][:, :],
              bass.AP(io["sink_i"], g * 512, [[8192, 64], [1, 512]]),
              inc=(rp_s, 16))
        for dt_ in range(4):
            wl = "wla" if dt_ % 2 == 0 else "wlb"
            c.await_guards("gpsimd", "wsl" + str(dt_ % 2))
            c.dma("gpsimd", t["wsl"][dt_ % 2][:, :],
                  bass.AP(io["Wk_T"], dt_ * 128,
                          [[512, 128], [128 * 512, 16], [1, 128]]),
                  inc=(wl, 16))
            c.w("tensor", wl)
            c.w("tensor", "act")
            c.w("tensor", "dve")
            for kt in range(NKT):
                c.mm(t["psA"][dt_ % 2][:, :],
                     t["wsl"][dt_ % 2][:, kt * 128:(kt + 1) * 128],
                     t["a_sb"][:, kt * 512:(kt + 1) * 512],
                     start=(kt == 0), stop=(kt == NKT - 1),
                     inc=("pe", 1) if kt == NKT - 1 else None)
            c.guard("wsl" + str(dt_ % 2), "pe")
            c.w("scalar", "pe")
            c.w("scalar", "dve")
            c.w("vector", rp_s)
            emit_norm_rope(
                c, t, t["psA"][dt_ % 2][:, :], t["kn_ws"], t["kn_wn"],
                t["cosb"][:, :], t["sinb"][:, :],
                t["Ksb"][0:64, dt_ * 8192 + g * 512:dt_ * 8192 + (g + 1) * 512],
                t["Ksb"][64:128, dt_ * 8192 + g * 512:dt_ * 8192 + (g + 1) * 512])
        c.guard("ropeb", "dve")
        for kt in range(NKT):
            wv = "vla" if kt % 2 == 0 else "vlb"
            c.await_guards("gpsimd", "wvb" + str(kt % 2))
            c.dma("gpsimd", t["wvb"][kt % 2][:, :],
                  bass.AP(io["Wv_T"], kt * 128 * 512, [[512, 128], [1, 512]]),
                  inc=(wv, 16))
            c.w("tensor", wv)
            if kt == 0:
                c.w("tensor", "act")
            for ts_ in range(4):
                c.mm(t["psC"][ts_ % 2][:, :] if ts_ < 2 else t["psD"][ts_ % 2][:, :],
                     t["a_sb"][:, kt * 512 + ts_ * 128:kt * 512 + (ts_ + 1) * 128],
                     t["wvb"][kt % 2][:, :],
                     start=(kt == 0), stop=(kt == NKT - 1),
                     inc=("pe", 1) if ts_ == 3 else None)
            c.guard("wvb" + str(kt % 2), "pe")
        c.w("scalar", "pe")
        for ts_ in range(4):
            ps = t["psC"][ts_ % 2] if ts_ < 2 else t["psD"][ts_ % 2]
            c.act(t["vsb"][:, :], ps[:, :], AF.Copy, inc=("act", 1))
            c.w("gpsimd", "act")
            c.dma("gpsimd",
                  bass.AP(io["v_hbm"], (g * 4 + ts_) * 16384,
                          [[128, 128], [1048576, 4], [1, 128]]),
                  t["vsb"][:, :], inc=("sta", 16))
            c.guard("vsb", "sta")
            c.w("scalar", "sta")

    for m in range(4):
        def h_ap(kt, m=m):
            return bass.AP(io["h_own"], kt * 128 * 2048 + m * 512,
                           [[2048, 128], [1, TC]])
        emit_ln(c, t, h_ap,
                lambda kt: t["a_sb"][:, kt * 512:(kt + 1) * 512])
        rp_s = "rpa" if m % 2 == 0 else "rpb"
        c.await_guards("gpsimd", "ropeb")
        c.dma("gpsimd", t["cosb"][:, :],
              bass.AP(io["cosq_i"], m * 512, [[2048, 64], [1, 512]]),
              inc=(rp_s, 16))
        c.dma("gpsimd", t["sinb"][:, :],
              bass.AP(io["sinq_i"], m * 512, [[2048, 64], [1, 512]]),
              inc=(rp_s, 16))
        c.w("vector", rp_s)
        for h in range(NH):
            wl = "wla" if h % 2 == 0 else "wlb"
            c.await_guards("gpsimd", "wsl" + str(h % 2))
            c.dma("gpsimd", t["wsl"][h % 2][:, :],
                  bass.AP(io["Wq_T"], h * 128,
                          [[2048, 128], [128 * 2048, 16], [1, 128]]),
                  inc=(wl, 16))
            c.w("tensor", wl)
            c.w("tensor", "act")
            c.w("tensor", "dve")
            for kt in range(NKT):
                c.mm(t["psA"][h % 2][:, :],
                     t["wsl"][h % 2][:, kt * 128:(kt + 1) * 128],
                     t["a_sb"][:, kt * 512:(kt + 1) * 512],
                     start=(kt == 0), stop=(kt == NKT - 1),
                     inc=("pe", 1) if kt == NKT - 1 else None)
            c.guard("wsl" + str(h % 2), "pe")
            c.w("scalar", "pe")
            c.w("scalar", "dve")
            emit_norm_rope(
                c, t, t["psA"][h % 2][:, :], t["qn_ws"], t["qn_wn"],
                t["cosb"][:, :], t["sinb"][:, :],
                t["q_n"][0:64, h * 2048 + m * 512:h * 2048 + (m + 1) * 512],
                t["q_n"][64:128, h * 2048 + m * 512:h * 2048 + (m + 1) * 512])
        c.guard("ropeb", "dve")


def emit_attn_core(c, io, t):
    for m in range(4):
        nb = (4 * m + 4) * 4
        for h in range(NH):
            gg = h // 4
            if h % 4 == 0:
                c.await_guards("gpsimd", "Vsl")
                c.dma("gpsimd", t["Vsl"][:, :nb * 128],
                      bass.AP(io["v_hbm"], gg * 1048576,
                              [[128, 128], [16384, nb], [1, 128]]),
                      inc=("vla", 16))
                c.w("tensor", "vla")
            qs = t["q_n"][:, h * 2048 + m * 512:h * 2048 + (m + 1) * 512]
            c.w("tensor", "dve")
            tanh_ct = {}
            pv_ct = {}
            for kb in range(nb):
                pp = kb % 2
                c.w("tensor", "act",
                    tanh_ct[kb - 2] if kb >= 2 else c.counts["act"])
                trailing = kb >= nb - 16
                c.mm(t["psA"][pp][:, :],
                     t["Ksb"][:, gg * 8192 + kb * 128:gg * 8192 + (kb + 1) * 128],
                     qs, start=True, stop=not trailing,
                     inc=None if trailing else ("pe", 1))
                if trailing:
                    i_ = kb - (nb - 16)
                    c.mm(t["psA"][pp][:, :], t["negI"][:, :],
                         t["Rm"][:, i_ * 512:(i_ + 1) * 512],
                         start=False, stop=True, inc=("pe", 1))
                c.w("scalar", "pe")
                c.act(t["Tsb"][pp][:, :], t["psA"][pp][:, :], AF.Tanh,
                      scale=t["kvals"][:, 3:4], inc=("act", 1))
                tanh_ct[kb] = c.counts["act"]
                c.w("scalar", "act")
                if kb >= 2:
                    c.w("scalar", "pe", pv_ct[kb - 2])
                c.act(t["Psb"][pp][:, :], t["Tsb"][pp][:, :], AF.Exp,
                      scale=t["kvals"][:, 2:3], inc=("act", 1))
                c.w("tensor", "act")
                c.mm(t["psB"][0][:1, :], t["ones128b"][:, :], t["Psb"][pp][:, :],
                     start=(kb == 0), stop=(kb == nb - 1), inc=None)
                c.mm(t["psB"][1][:, :],
                     t["Vsl"][:, kb * 128:(kb + 1) * 128], t["Psb"][pp][:, :],
                     start=(kb == 0), stop=(kb == nb - 1),
                     inc=("pe", 1))
                pv_ct[kb] = c.counts["pe"]
            if h % 4 == 3:
                c.guard("Vsl", "pe")
            c.w("vector", "pe")
            c.vcopy(t["lrow"][:1, :], t["psB"][0][:1, :], inc=("dve", 1))
            c.w("vector", "dve")
            c.recip(t["lrow"][:1, :], t["lrow"][:1, :], inc=("dve", 1))
            c.w("tensor", "dve")
            c.mm(t["psC"][0][:, :], t["ones1f"][:1, :], t["lrow"][:1, :],
                 True, True, inc=("pe", 1))
            c.w("vector", "pe")
            c.vcopy(t["rbcs"][:, :], t["psC"][0][:, :], inc=("dve", 1))
            c.w("vector", "dve")
            c.tt(t["y_m"][:, h * 512:(h + 1) * 512], t["psB"][1][:, :],
                 t["rbcs"][:, :], ALU.mult, inc=("dve", 1))
        for co in range(16):
            bi = co % 2
            wl = "wla" if bi == 0 else "wlb"
            st_ = "sta" if bi == 0 else "stb"
            rl = "rla" if bi == 0 else "rlb"
            c.await_guards("gpsimd", "wslo" + str(bi))
            c.dma("gpsimd", t["wsl"][bi][:, :],
                  bass.AP(io["Wo_T"], co * 128,
                          [[2048, 128], [128 * 2048, 16], [1, 128]]),
                  inc=(wl, 16))
            c.await_guards("gpsimd", "h1t" + str(bi))
            c.dma("gpsimd", t["h1t"][bi][:, :],
                  bass.AP(io["h_own"], co * 128 * 2048 + m * 512,
                          [[2048, 128], [1, TC]]),
                  inc=(rl, 16))
            c.w("tensor", wl)
            c.w("tensor", "dve")
            for h in range(NH):
                c.mm(t["psD"][bi][:, :],
                     t["wsl"][bi][:, h * 128:(h + 1) * 128],
                     t["y_m"][:, h * 512:(h + 1) * 512],
                     start=(h == 0), stop=(h == NH - 1),
                     inc=("pe", 1) if h == NH - 1 else None)
            c.guard("wslo" + str(bi), "pe")
            c.w("vector", "pe")
            c.w("vector", rl)
            c.await_guards("vector", "h2t" + str(bi))
            c.stt(t["h2t"][bi][:, :], t["psD"][bi][:, :], t["bo"][:, co:co + 1],
                  t["h1t"][bi][:, :], ALU.add, ALU.add, inc=("dve", 1))
            c.guard("h1t" + str(bi), "dve")
            c.w("gpsimd", "dve")
            c.dma("gpsimd",
                  bass.AP(io["h2_own"], co * 128 * 2048 + m * 512,
                          [[2048, 128], [1, TC]]),
                  t["h2t"][bi][:, :], inc=(st_, 16))
            c.guard("h2t" + str(bi), st_)


T_FULL_B = 8192


def build_attn_program():
    nc = bass.Bass("TRN2", target_bir_lowering=False)
    stk = ExitStack()
    io = {}
    io["h_full"] = nc.dram_tensor("h_full", [D, T_FULL_B], F32,
                                  kind="ExternalInput")
    io["h_own"] = nc.dram_tensor("h_own", [D, 2048], F32, kind="ExternalInput")
    io["h2_own"] = nc.dram_tensor("h2_own", [D, 2048], F32,
                                  kind="ExternalOutput")
    for nm, shp in (("Wq_T", [D, D]), ("Wk_T", [D, 512]), ("Wv_T", [D, 512]),
                    ("Wo_T", [D, D])):
        io[nm] = nc.dram_tensor(nm, shp, F32, kind="ExternalInput")
    io["bo_i"] = nc.dram_tensor("bo_i", [128, 16], F32, kind="ExternalInput")
    io["qn_w"] = nc.dram_tensor("qn_w", [64, 2], F32, kind="ExternalInput")
    io["kn_w"] = nc.dram_tensor("kn_w", [64, 2], F32, kind="ExternalInput")
    io["negI_i"] = nc.dram_tensor("negI_i", [128, 128], BF16,
                                  kind="ExternalInput")
    io["Rm_i"] = nc.dram_tensor("Rm_i", [128, 16 * 512], BF16,
                                kind="ExternalInput")
    io["cosk_i"] = nc.dram_tensor("cosk_i", [64, 8192], F32,
                                  kind="ExternalInput")
    io["sink_i"] = nc.dram_tensor("sink_i", [64, 8192], F32,
                                  kind="ExternalInput")
    io["cosq_i"] = nc.dram_tensor("cosq_i", [64, 2048], F32,
                                  kind="ExternalInput")
    io["sinq_i"] = nc.dram_tensor("sinq_i", [64, 2048], F32,
                                  kind="ExternalInput")
    io["v_hbm"] = nc.dram_tensor("v_hbm", [4 * 64 * 128 * 128], BF16)

    sems = {n: stk.enter_context(nc.semaphore(n)) for n in ATTN_SEMS}
    t = alloc_attn_common(nc, stk)
    alloc_psum(nc, stk, t)
    b = Builder(nc, sems)

    prep_stk = ExitStack()
    alloc_attn_prep(nc, prep_stk, t)

    def prog_prep(c):
        emit_attn_consts(c, io, t)
        c.w("tensor", "pre")
        c.w("scalar", "pre")
        c.w("gpsimd", "pre")
        emit_attn_prep(c, io, t)

    b.block(prog_prep)
    prep_stk.close()
    for k in ("a_sb", "stat", "rbc", "mrbc", "hbuf", "hsq", "wsl", "wvb",
              "cosb", "sinb", "cosq", "sinq", "kn0", "rp", "rrow", "vsb"):
        t.pop(k, None)
    core_stk = ExitStack()
    alloc_attn_core(nc, core_stk, t)

    def prog_core(c):
        c.barrier()
        c.dma("gpsimd", t["negI"][:, :], io["negI_i"].ap(), inc=("pre", 16))
        c.dma("gpsimd", t["bo"][:, :], io["bo_i"].ap(), inc=("pre", 16))
        c.dma("gpsimd", t["Rm"][:, :], io["Rm_i"].ap(), inc=("pre", 16))
        c.w("tensor", "pre")
        c.w("vector", "pre")
        emit_attn_core(c, io, t)
        c.w("gpsimd", "sta")
        c.w("gpsimd", "stb")

    b.block(prog_core)
    core_stk.close()
    stk.close()
    return nc


def stage_attn_inputs(ins, j):
    Wq = np.asarray(ins["L0_attn_Wq"], np.float32)
    Wk = np.asarray(ins["L0_attn_Wk"], np.float32)
    Wv = np.asarray(ins["L0_attn_Wv"], np.float32)
    Wo = np.asarray(ins["L0_attn_Wo"], np.float32)
    inv_freq = 1.0 / 10000.0 ** (np.arange(0, HD, 2, dtype=np.float32) / HD)
    pos_full = np.arange(T_FULL_B, dtype=np.float32)
    own_tok = np.concatenate(
        [np.arange((4 * mm + j) * 512, (4 * mm + j) * 512 + 512)
         for mm in range(4)]).astype(np.float32)
    frf = pos_full[:, None] * inv_freq[None, :]
    fro = own_tok[:, None] * inv_freq[None, :]
    Rm = np.zeros((128, 16 * 512), dtype=NPBF16)
    Kk = np.arange(128)[:, None]
    tq = np.arange(512)[None, :]
    for i in range(16):
        d_ = 128 * (i - 4 * j)
        Rm[:, i * 512:(i + 1) * 512] = (Kk - tq > -d_).astype(NPBF16)
    return {
        "Wq_T": np.ascontiguousarray(Wq.T),
        "Wk_T": np.ascontiguousarray(Wk.T),
        "Wv_T": np.ascontiguousarray(Wv.T),
        "Wo_T": np.ascontiguousarray(Wo.T),
        "bo_i": np.ascontiguousarray(
            np.asarray(ins["L0_attn_bo"], np.float32).reshape(16, 128).T),
        "qn_w": np.ascontiguousarray(
            np.asarray(ins["L0_attn_q_norm_w"], np.float32).reshape(2, 64).T),
        "kn_w": np.ascontiguousarray(
            np.asarray(ins["L0_attn_k_norm_w"], np.float32).reshape(2, 64).T),
        "negI_i": (NEG_BIG * np.eye(128, dtype=np.float32)).astype(NPBF16),
        "Rm_i": Rm,
        "cosk_i": np.ascontiguousarray(np.cos(frf).T.astype(np.float32)),
        "sink_i": np.ascontiguousarray(np.sin(frf).T.astype(np.float32)),
        "cosq_i": np.ascontiguousarray(np.cos(fro).T.astype(np.float32)),
        "sinq_i": np.ascontiguousarray(np.sin(fro).T.astype(np.float32)),
    }


def own_token_index(j):
    return np.concatenate(
        [np.arange((4 * m + j) * 512, (4 * m + j + 1) * 512) for m in range(4)])



def kernel(**inputs):
    x = np.asarray(inputs["L0_attn_x"], np.float32)
    staged_f = stage_ffn_inputs(inputs)

    nc_f = build_ffn_program(2048)
    in_maps = []
    xf = x.reshape(-1, D)
    for c_ in range(8):
        m = dict(staged_f)
        m["h_in"] = np.ascontiguousarray(xf[c_ * 2048:(c_ + 1) * 2048].T)
        in_maps.append(m)
    res1 = run_bass_kernel_spmd(nc_f, in_maps, core_ids=list(range(8)))
    h1 = np.empty((2, T_FULL_B, D), np.float32)
    for c_ in range(8):
        h1.reshape(-1, D)[c_ * 2048:(c_ + 1) * 2048] = res1.results[c_]["h_out"].T

    nc_a = build_attn_program()
    in_maps = []
    for c_ in range(8):
        b_, j_ = c_ // 4, c_ % 4
        m = stage_attn_inputs(inputs, j_)
        m["h_full"] = np.ascontiguousarray(h1[b_].T)
        idx = own_token_index(j_)
        m["h_own"] = np.ascontiguousarray(h1[b_][idx].T)
        in_maps.append(m)
    res2 = run_bass_kernel_spmd(nc_a, in_maps, core_ids=list(range(8)))
    h2 = np.empty((2, T_FULL_B, D), np.float32)
    for c_ in range(8):
        b_, j_ = c_ // 4, c_ % 4
        idx = own_token_index(j_)
        h2[b_][idx] = res2.results[c_]["h2_own"].T

    in_maps = []
    h2f = h2.reshape(-1, D)
    for c_ in range(8):
        m = dict(staged_f)
        m["h_in"] = np.ascontiguousarray(h2f[c_ * 2048:(c_ + 1) * 2048].T)
        in_maps.append(m)
    res3 = run_bass_kernel_spmd(nc_f, in_maps, core_ids=list(range(8)))
    out = np.empty((2, T_FULL_B, D), np.float32)
    for c_ in range(8):
        out.reshape(-1, D)[c_ * 2048:(c_ + 1) * 2048] = \
            res3.results[c_]["h_out"].T
    return out

